# version 47
# baseline (speedup 1.0000x reference)
import contextlib
import numpy as np
import ml_dtypes
import concourse.bass as bass
import concourse.mybir as mybir
from concourse.bass_utils import run_bass_kernel_spmd

F32, BF16 = mybir.dt.float32, mybir.dt.bfloat16
AF, OP = mybir.ActivationFunctionType, mybir.AluOpType
AX = mybir.AxisListType

NCORES = 8
BSEQ = 2
L = 2048
T = 512
NCH = BSEQ * (L // T)
D = 1024
DI = 2048
HF = 4096
NST = 16
KB = D // 128
DBL = DI // 128
HT = HF // 128
DEBUG = False



class _Op:
    __slots__ = ("eng", "fn", "deps", "val", "need", "amt", "stream")
    def __init__(self, eng, fn, deps, amt):
        self.eng, self.fn, self.deps, self.amt = eng, fn, deps, amt
        self.val = None
        self.need = False


class Plan:
    ENGS = ("pe", "act", "dve", "pool", "syncq", "gpq")
    TOTAL_STREAMS = ("const_s", "const_g")

    def __init__(self):
        self.ops = []
        self.track = {}

    def add(self, eng, fn, reads=(), writes=(), deps=(), stream=None):
        dl = list(deps)
        for k in reads:
            w, _ = self.track.get(k, (None, []))
            if w is not None:
                dl.append(w)
        for k in writes:
            w, rs = self.track.get(k, (None, []))
            if w is not None:
                dl.append(w)
            dl.extend(rs)
        amt = 16 if eng in ("syncq", "gpq") else 1
        op = _Op(eng, fn, dl, amt)
        op.stream = stream
        if eng in ("syncq", "gpq"):
            assert stream is not None, "DMA ops need a stream label"
        self.ops.append(op)
        for k in reads:
            w, rs = self.track.get(k, (None, []))
            rs.append(op)
            self.track[k] = (w, rs)
        for k in writes:
            self.track[k] = (op, [])
        return op

    def finalize(self):
        for op in self.ops:
            if op.eng in ("syncq", "gpq"):
                op.need = True
            for d in op.deps:
                if d.eng != op.eng:
                    d.need = True
        cnt = {}
        scnt = {}
        for op in self.ops:
            if op.eng in ("syncq", "gpq"):
                scnt[op.stream] = scnt.get(op.stream, 0) + 16
                op.val = scnt[op.stream]
            elif op.need:
                cnt[op.eng] = cnt.get(op.eng, 0) + 1
                op.val = cnt[op.eng]
        for op in self.ops:
            if op.eng in ("syncq", "gpq") and op.stream in self.TOTAL_STREAMS:
                op.val = scnt[op.stream]
        self.stream_totals = scnt
        self.totals = cnt
        return cnt


def build_nc():
    nc = bass.Bass("TRN2", target_bir_lowering=False, debug=False)
    ap = {}

    def din(name, shape):
        ap[name] = nc.dram_tensor(name, list(shape), F32, kind="ExternalInput").ap()

    din("Xc", (NCH, 1024, T))
    din("WinS", (32, 1024, 128))
    din("w1T", (1024, 4096))
    din("w2T", (4096, 1024))
    din("w1o", (4096, 1024))
    din("w2o", (1024, 4096))
    din("WoutS", (8, 2048, 128))
    din("WxTr", (128, 16, 96))
    din("WdtT", (64, 2048))
    din("Ar", (128, 256))
    din("convw", (128, 64))
    din("convb", (128, 16))
    din("bdt", (128, 16))
    din("Dpr", (128, 16))
    din("b1r", (128, 32))
    din("b2c", (128, 8))
    din("eye", (32, 4096))
    din("ones128", (128, 1))
    din("ones1x128", (1, 128))
    ap["Oc"] = nc.dram_tensor("Oc", [NCH, 1024, T], F32, kind="ExternalOutput").ap()
    if DEBUG:
        for nm, shp in [("d_xn", (128, 8, T)), ("d_g", (128, 16, T)), ("d_xcv", (128, 16, T)),
                        ("d_dt", (128, 16, T)), ("d_dtx", (128, 16, T)), ("d_y", (128, 16, T)),
                        ("d_ymix", (128, 16, T)), ("d_xffn", (128, 8, T)), ("d_rows", (1, 8, T)),
                        ("d_bc", (32, T)), ("d_dtl", (64, T)), ("d_alpha1", (128, 32)),
                        ("d_alpha2", (128, 8)), ("d_A", (128, 256)), ("d_h1", (128, 8, T)),
                        ("d_hstate", (128, 256)), ("d_hcols", (128, 256)), ("d_guards", (128, 128)), ("d_hsnap", (128, 256)),
                        ("d_xffnA", (NCH, 128, 8, T)), ("d_xfinA", (NCH, 128, 8, T)), ("d_ymixA", (NCH, 128, 16, T)),
                        ("d_ffnp", (NCH, 4, 128, T)), ("d_h1A", (NCH, 4, 128, 8, T))]:
            ap[nm] = nc.dram_tensor(nm, list(shp), F32, kind="ExternalOutput").ap()
    ap["s1S"] = nc.dram_tensor("s1S", [32, 1024, 128], BF16, kind="Internal").ap()
    ap["winB"] = nc.dram_tensor("winB", [32, 1024, 128], BF16, kind="Internal").ap()
    ap["woutB"] = nc.dram_tensor("woutB", [8, 2048, 128], BF16, kind="Internal").ap()
    ap["s2S"] = nc.dram_tensor("s2S", [4096, 1024], BF16, kind="Internal").ap()

    es = contextlib.ExitStack()

    def sb(name, shape, dt):
        return es.enter_context(nc.sbuf_tensor(name, list(shape), dt))

    def psum(name, shape):
        return es.enter_context(nc.psum_tensor(name, list(shape), F32))

    plan = Plan()

    with es:
        ones128_bf = sb("ones128_bf", (128, 1), BF16)
        ones1x128_f = sb("ones1x128_f", (1, 128), F32)
        E_sb = sb("E_sb", (32, 4096), BF16)
        convw_sb = sb("convw_sb", (128, 64), F32)
        convb_sb = sb("convb_sb", (128, 16), F32)
        bdt_sb = sb("bdt_sb", (128, 16), F32)
        Dp_sb = sb("Dp_sb", (128, 16), F32)
        b1c_sb = sb("b1c_sb", (128, 32), F32)
        b2c_sb = sb("b2c_sb", (128, 8), F32)
        alpha1c = sb("alpha1c", (128, 32), F32)
        alpha2c = sb("alpha2c", (128, 8), F32)
        alpha1r = sb("alpha1r", (128, 32), F32)
        alpha2r = sb("alpha2r", (128, 8), F32)
        wx_sb = sb("wx_sb", (128, 16, 96), BF16)
        wdt_sb = sb("wdt_sb", (64, 2048), BF16)
        A_load = sb("A_load", (128, 256), F32)
        A_sb = sb("A_sb", (128, 256), F32)

        P = plan

        def dma(eng, out_ap, in_ap, reads=(), writes=(), deps=(), stream=None):
            if stream is None:
                stream = "const_s" if eng == "syncq" else "const_g"
            return P.add(eng, lambda e, o=out_ap, i=in_ap: e.dma_start(out=o, in_=i),
                         reads=reads, writes=writes, deps=deps, stream=stream)

        def dmaD(eng, out_ap, in_ap, reads=(), writes=(), deps=()):
            return dma(eng, out_ap, in_ap, reads=reads, writes=writes, deps=deps,
                       stream="dump")

        dma("gpq", ones128_bf[:], ap["ones128"], writes=[("ones128_bf",)])
        dma("syncq", ones1x128_f[:], ap["ones1x128"], writes=[("ones1x128_f",)])
        dma("gpq", E_sb[:], ap["eye"], writes=[("E_sb",)])
        dma("syncq", convw_sb[:], ap["convw"], writes=[("convw",)])
        dma("syncq", convb_sb[:], ap["convb"], writes=[("convb",)])
        dma("syncq", bdt_sb[:], ap["bdt"], writes=[("bdt",)])
        dma("syncq", Dp_sb[:], ap["Dpr"], writes=[("Dp",)])
        dma("syncq", b1c_sb[:], ap["b1r"], writes=[("b1c",)])
        dma("syncq", b2c_sb[:], ap["b2c"], writes=[("b2c",)])
        dma("gpq", wx_sb[:], ap["WxTr"], writes=[("wx",)])
        dma("gpq", wdt_sb[:], ap["WdtT"], writes=[("wdt",)])
        dma("syncq", A_load[:], ap["Ar"], writes=[("A_load",)])

        P.add("act", lambda e: e.activation(out=A_sb[:], in_=A_load[:], func=AF.Exp),
              reads=[("A_load",)], writes=[("A_sb",)])
        P.add("dve", lambda e: e.tensor_scalar_mul(out=A_sb[:], in0=A_sb[:], scalar1=-1.0),
              reads=[("A_sb",)], writes=[("A_sb",)])

        WinR_ = ap["WinS"].rearrange("s (kb p) m -> s p kb m", p=128)
        WoutR_ = ap["WoutS"].rearrange("s (k p) mm -> s p k mm", p=128)

        x_sb = sb("x_sb", (128, 8, T), F32)
        xn_sb = sb("xn_sb", (128, 8, T), BF16)
        xb_t = sb("xb_t", (128, 2, T), BF16)
        x2b_t = sb("x2b_t", (128, 2, T), BF16)
        rowsA = sb("rowsA", (128, T), F32)
        rowsB = sb("rowsB", (128, T), F32)
        rowsC = sb("rowsC", (128, T), F32)
        _rowmap = [(0, 0), (0, 32), (1, 32), (2, 32), (0, 64), (1, 0), (2, 0)]

        def rowap(i):
            t, p = _rowmap[i]
            buf = (rowsA, rowsB, rowsC)[t]
            return buf[p:p + 1, :]
        ln_tmp = sb("ln_tmp", (128, T), F32)
        ps_a = psum("ps_a", (128, 2, T))
        ps_b = psum("ps_b", (128, 2, T))
        ps_m = psum("ps_m", (128, 2, T))
        ps_bc = psum("ps_bc", (128, 2, T))

        XcR = ap["Xc"].rearrange("c (kb p) f -> c p kb f", p=128)

        def emit_ln(ci, tail0=None):
            for kb in range(KB):
                par = kb % 2
                P.add("dve", lambda e, kb=kb, par=par: e.tensor_copy(
                    out=xb_t[:, par, :], in_=x_sb[:, kb, :]),
                    reads=[("x", kb)], writes=[("xb_t", par)])
                P.add("act", lambda e, kb=kb, par=par: e.activation(
                    out=x2b_t[:, par, :], in_=x_sb[:, kb, :], func=AF.Square),
                    reads=[("x", kb)], writes=[("x2b_t", par)])
                P.add("pe", lambda e, kb=kb, par=par: e.matmul(
                    out=ps_m[0:1, 0, :], lhsT=ones128_bf[:], rhs=xb_t[:, par, :],
                    start=(kb == 0), stop=(kb == KB - 1)),
                    reads=[("xb_t", par), ("ones128_bf",)], writes=[("psm", 0)])
                P.add("pe", lambda e, kb=kb, par=par: e.matmul(
                    out=ps_m[0:1, 1, :], lhsT=ones128_bf[:], rhs=x2b_t[:, par, :],
                    start=(kb == 0), stop=(kb == KB - 1)),
                    reads=[("x2b_t", par), ("ones128_bf",)], writes=[("psm", 1)])
            P.add("dve", lambda e: e.tensor_scalar_mul(
                out=rowap(0), in0=ps_m[0:1, 0, :], scalar1=1.0 / D),
                reads=[("psm", 0)], writes=[("row", 0)])
            P.add("dve", lambda e: e.tensor_scalar_mul(
                out=rowap(1), in0=ps_m[0:1, 1, :], scalar1=1.0 / D),
                reads=[("psm", 1)], writes=[("row", 1)])
            P.add("act", lambda e: e.activation(
                out=rowap(2), in_=rowap(0), func=AF.Square),
                reads=[("row", 0)], writes=[("row", 2)])
            P.add("dve", lambda e: e.scalar_tensor_tensor(
                out=rowap(3), in0=rowap(1), scalar=1e-5,
                in1=rowap(2), op0=OP.add, op1=OP.subtract),
                reads=[("row", 1), ("row", 2)], writes=[("row", 3)])
            P.add("act", lambda e: e.activation(
                out=rowap(4), in_=rowap(3), func=AF.Ln),
                reads=[("row", 3)], writes=[("row", 4)])
            P.add("act", lambda e: e.activation(
                out=rowap(5), in_=rowap(4), func=AF.Exp, scale=-0.5),
                reads=[("row", 4)], writes=[("row", 5)])
            P.add("dve", lambda e: e.tensor_mul(
                out=rowap(6), in0=rowap(0), in1=rowap(5)),
                reads=[("row", 0), ("row", 5)], writes=[("row", 6)])
            P.add("pe", lambda e: e.matmul(
                out=ps_m[:, 0, :], lhsT=ones1x128_f[:], rhs=rowap(5),
                start=True, stop=True),
                reads=[("row", 5), ("ones1x128_f",)], writes=[("psm", 0)])
            P.add("pe", lambda e: e.matmul(
                out=ps_m[:, 1, :], lhsT=ones1x128_f[:], rhs=rowap(6),
                start=True, stop=True),
                reads=[("row", 6), ("ones1x128_f",)], writes=[("psm", 1)])
            for kb in range(KB):
                P.add("dve", lambda e, kb=kb: e.tensor_mul(
                    out=ln_tmp[:], in0=x_sb[:, kb, :], in1=ps_m[:, 0, :]),
                    reads=[("x", kb), ("psm", 0)], writes=[("ln_tmp",)])
                P.add("dve", lambda e, kb=kb: e.tensor_sub(
                    out=xn_sb[:, kb, :], in0=ln_tmp[:], in1=ps_m[:, 1, :]),
                    reads=[("ln_tmp",), ("psm", 1)], writes=[("xn", kb)])
                if kb == KB - 2 and tail0 is not None:
                    tail0()

        dma("syncq", x_sb[:], XcR[0], writes=[("x", kb) for kb in range(KB)],
            stream="xload")
        emit_ln(0)

        with contextlib.ExitStack() as prep_es:
            ptile_f = prep_es.enter_context(nc.sbuf_tensor("ptile_f", [128, 2, 4096], F32))
            ptile_s = prep_es.enter_context(nc.sbuf_tensor("ptile_s", [128, 2, 4096], BF16))

            winB_w = ap["winB"].rearrange("s (kb p) mm -> s p kb mm", p=128)
            woutB_w = ap["woutB"].rearrange("s (k p) mm -> s p k mm", p=128)
            for m in range(32):
                par = m % 2
                view = ptile_s[:, par, 0:1024].rearrange("p (kb mm) -> p kb mm", mm=128)
                dma("gpq", view, WinR_[m], writes=[("ptile_s", par)],
                    stream="pwc%d" % par)
                dma("syncq", winB_w[m], view, reads=[("ptile_s", par)],
                    writes=[("winB", m)], stream="pws%d" % par)
            for m in range(8):
                par = m % 2
                view = ptile_s[:, par, 0:2048].rearrange("p (k mm) -> p k mm", mm=128)
                dma("gpq", view, WoutR_[m], writes=[("ptile_s", par)],
                    stream="pwc%d" % par)
                dma("syncq", woutB_w[m], view, reads=[("ptile_s", par)],
                    writes=[("woutB", m)], stream="pws%d" % par)

            s1S_w = ap["s1S"].rearrange("ht (kb p) mm -> kb p ht mm", p=128)
            for kbp in range(8):
                par = kbp % 2
                dma("syncq", ptile_f[:, par, :], ap["w1T"][128 * kbp:128 * (kbp + 1), :],
                    writes=[("ptile_f", par)], stream="pf%d" % par)
                P.add("act", lambda e, par=par: e.activation(
                    out=ptile_s[:, par, :], in_=ptile_f[:, par, :], func=AF.Sign),
                    reads=[("ptile_f", par)], writes=[("ptile_s", par)])
                dma("gpq", s1S_w[kbp],
                    ptile_s[:, par, :].rearrange("p (ht mm) -> p ht mm", mm=128),
                    reads=[("ptile_s", par)], writes=[("s1S", kbp)], stream="prep_out%d" % par)
            for bb in range(32):
                par = bb % 2
                dma("syncq", ptile_f[:, par, 0:1024], ap["w2T"][128 * bb:128 * (bb + 1), :],
                    writes=[("ptile_f", par)], stream="pf%d" % par)
                P.add("act", lambda e, par=par: e.activation(
                    out=ptile_s[:, par, 0:1024], in_=ptile_f[:, par, 0:1024], func=AF.Sign),
                    reads=[("ptile_f", par)], writes=[("ptile_s", par)])
                dma("gpq", ap["s2S"][128 * bb:128 * (bb + 1), :], ptile_s[:, par, 0:1024],
                    reads=[("ptile_s", par)], writes=[("s2S", bb)], stream="prep_out%d" % par)
            for ht in range(32):
                par = ht % 2
                dma("syncq", ptile_f[:, par, 0:1024], ap["w1o"][128 * ht:128 * (ht + 1), :],
                    writes=[("ptile_f", par)], stream="pf%d" % par)
                P.add("dve", lambda e, ht=ht, par=par: e.reduce_sum(
                    out=alpha1r[:, ht:ht + 1], in_=ptile_f[:, par, 0:1024], axis=AX.X,
                    apply_absolute_value=True),
                    reads=[("ptile_f", par)], writes=[("alpha1r", ht)])
            for m in range(8):
                par = m % 2
                dma("syncq", ptile_f[:, par, :], ap["w2o"][128 * m:128 * (m + 1), :],
                    writes=[("ptile_f", par)], stream="pf%d" % par)
                P.add("dve", lambda e, m=m, par=par: e.reduce_sum(
                    out=alpha2r[:, m:m + 1], in_=ptile_f[:, par, :], axis=AX.X,
                    apply_absolute_value=True),
                    reads=[("ptile_f", par)], writes=[("alpha2r", m)])
            P.add("dve", lambda e: e.tensor_scalar_mul(
                out=alpha1c[:], in0=alpha1r[:], scalar1=1.0 / 1024.0),
                reads=[("alpha1r", ht) for ht in range(32)], writes=[("alpha1c",)])
            P.add("dve", lambda e: e.tensor_scalar_mul(
                out=alpha2c[:], in0=alpha2r[:], scalar1=1.0 / 4096.0),
                reads=[("alpha2r", m) for m in range(8)], writes=[("alpha2c",)])
            last_po = {}
            for _o in P.ops:
                if getattr(_o, "stream", None) in ("prep_out0", "prep_out1",
                                                   "pws0", "pws1"):
                    last_po[_o.stream] = _o
            scratch_done = list(last_po.values())
            prep_tail = [P.ops[-1]]
            last_per_eng = {}
            for op in P.ops:
                last_per_eng[op.eng] = op
            prep_barrier = list(last_per_eng.values())

        xc_buf = sb("xc_buf", (128, 2, T + 3), BF16)
        xtail = sb("xtail", (128, 16, 3), BF16)
        sp_buf = sb("sp_buf", (128, 2, T), F32)
        conv_acc = sb("conv_acc", (128, 2, T), BF16)
        g_sb = sb("g_sb", (128, 16, T), BF16)
        xcv_sb = sb("xcv_sb", (128, 16, T), BF16)
        dt_sb = sb("dt_sb", (128, 16, T), BF16)
        dtx_sb = sb("dtx_sb", (128, 16, T), BF16)
        dtl_sb = sb("dtl_sb", (64, T), BF16)
        bc_sb = sb("bc_sb", (32, T), BF16)
        Bbc_sb = sb("Bbc_sb", (128, 2, T), BF16)
        Cbc_sb = sb("Cbc_sb", (128, 2, T), BF16)
        dA_buf = sb("dA_buf", (128, 4, T), F32)
        dBx_buf = sb("dBx_buf", (128, 4, T), BF16)
        H_buf = sb("H_buf", (128, 8, T), BF16)
        HC_buf = sb("HC_buf", (128, 4, T), BF16)
        y_sb = sb("y_sb", (128, 16, T), BF16)
        h_guard1 = sb("h_guard1", (128, 8), F32) if DEBUG else None
        h_state = sb("h_state", (128, 2, 256), F32)
        h_guard2 = sb("h_guard2", (128, 8), F32) if DEBUG else None
        h1grp = sb("h1grp", (128, 8, T), BF16)
        ts_buf = sb("ts_buf", (128, 2, T), F32)
        win_sb = sb("win_sb", (128, 2, 8, 128), BF16)
        s1_sb = sb("s1_sb", (128, 2, 8, 128), BF16)
        s2_sb = sb("s2_sb", (128, 2, 8, 128), BF16)
        wout_sb = sb("wout_sb", (128, 2, 16, 128), BF16)
        hdbg_sb = sb("hdbg_sb", (128, 256), F32) if DEBUG else None

        OcR = ap["Oc"].rearrange("c (kb p) f -> c p kb f", p=128)
        WinR = ap["winB"].rearrange("s (kb p) m -> s p kb m", p=128)
        s1R = ap["s1S"].rearrange("ht (kb p) mm -> ht p kb mm", p=128)
        s2R = ap["s2S"].rearrange("(b p) m -> p b m", p=128)
        WoutR = ap["woutB"].rearrange("s (k p) mm -> s p k mm", p=128)

        barrier_deps = prep_barrier

        for ci in range(NCH):
            seq_start = (ci % (L // T)) == 0

            def emit_tailin(db, seq_start=seq_start):
                if seq_start:
                    P.add("dve", lambda e, db=db: e.memset(xc_buf[:, db % 2, 0:3], 0.0),
                          writes=[("xcpad", db % 2)])
                else:
                    P.add("dve", lambda e, db=db: e.tensor_copy(
                        out=xc_buf[:, db % 2, 0:3], in_=xtail[:, db, :]),
                        reads=[("xtail", db)], writes=[("xcpad", db % 2)])

            if ci > 0:
                dma("syncq", x_sb[:], XcR[ci],
                    writes=[("x", kb) for kb in range(KB)], stream="xload")
                emit_ln(ci, tail0=lambda: emit_tailin(0))
            else:
                emit_tailin(0)

            WIN_BANKS = [(ps_a, 0, "psa"), (ps_a, 1, "psa"),
                         (ps_m, 0, "psm"), (ps_m, 1, "psm")]
            for m in range(16):
                wpar = m % 2
                dma("syncq", win_sb[:, wpar, :, :], WinR[m],
                    reads=[("winB", m)], writes=[("win", wpar)],
                    stream="win%d" % wpar)
                pst, psl, psk = WIN_BANKS[m % 4]
                if True:
                    for kb in range(KB):
                        P.add("pe", lambda e, wpar=wpar, pst=pst, psl=psl, kb=kb: e.matmul(
                            out=pst[:, psl, :],
                            lhsT=win_sb[:, wpar, kb, :],
                            rhs=xn_sb[:, kb, :],
                            start=(kb == 0), stop=(kb == KB - 1)),
                            reads=[("win", wpar), ("xn", kb)], writes=[(psk, psl)])
                    if m < 16:
                        db = m
                        P.add("act", lambda e, db=db, pst=pst, psl=psl: e.activation(
                            out=xc_buf[:, db % 2, 3:3 + T], in_=pst[:, psl, :], func=AF.Copy),
                            reads=[(psk, psl)], writes=[("xcmain", db % 2)])
                        if db + 1 < 16:
                            emit_tailin(db + 1)
                        P.add("dve", lambda e, db=db: e.tensor_scalar_mul(
                            out=sp_buf[:, db % 2, :], in0=xc_buf[:, db % 2, 0:T],
                            scalar1=convw_sb[:, 4 * db + 0:4 * db + 1]),
                            reads=[("xcmain", db % 2), ("xcpad", db % 2), ("convw",)],
                            writes=[("sp", db % 2)])
                        P.add("dve", lambda e, db=db: e.scalar_tensor_tensor(
                            out=conv_acc[:, db % 2, :], in0=xc_buf[:, db % 2, 1:1 + T],
                            scalar=convw_sb[:, 4 * db + 1:4 * db + 2],
                            in1=sp_buf[:, db % 2, :], op0=OP.mult, op1=OP.add),
                            reads=[("xcmain", db % 2), ("xcpad", db % 2), ("sp", db % 2), ("convw",)],
                            writes=[("cacc", db % 2)])
                        P.add("dve", lambda e, db=db: e.scalar_tensor_tensor(
                            out=sp_buf[:, db % 2, :], in0=xc_buf[:, db % 2, 2:2 + T],
                            scalar=convw_sb[:, 4 * db + 2:4 * db + 3],
                            in1=conv_acc[:, db % 2, :], op0=OP.mult, op1=OP.add),
                            reads=[("xcmain", db % 2), ("cacc", db % 2), ("convw",)],
                            writes=[("sp", db % 2)])
                        P.add("dve", lambda e, db=db: e.scalar_tensor_tensor(
                            out=conv_acc[:, db % 2, :], in0=xc_buf[:, db % 2, 3:3 + T],
                            scalar=convw_sb[:, 4 * db + 3:4 * db + 4],
                            in1=sp_buf[:, db % 2, :], op0=OP.mult, op1=OP.add),
                            reads=[("xcmain", db % 2), ("sp", db % 2), ("convw",)],
                            writes=[("cacc", db % 2)])
                        if ci % (L // T) != (L // T) - 1:
                            P.add("dve", lambda e, db=db: e.tensor_copy(
                                out=xtail[:, db, :], in_=xc_buf[:, db % 2, T:T + 3]),
                                reads=[("xcmain", db % 2)], writes=[("xtail", db)])
                        P.add("act", lambda e, db=db: e.activation(
                            out=xcv_sb[:, db, :], in_=conv_acc[:, db % 2, :], func=AF.Silu,
                            bias=convb_sb[:, db:db + 1]),
                            reads=[("cacc", db % 2), ("convb",)], writes=[("xcv", db)])
            def emit_zpair_dma(p):
                wpar = p % 2
                view = ap["winB"][16 + 2 * p:18 + 2 * p].rearrange(
                    "s (kb p) m -> p (s kb) m", p=128)
                dma("syncq", wout_sb[:, wpar, :, :], view,
                    reads=[("winB", 16 + 2 * p), ("winB", 17 + 2 * p)],
                    writes=[("wout", wpar)], stream="wout%d" % wpar)

            def emit_zprefetch():
                emit_zpair_dma(0)
                emit_zpair_dma(1)

            def emit_zhalf():
                for m in range(16, 32):
                    p = (m - 16) // 2
                    half = (m - 16) % 2
                    wpar = p % 2
                    pst, psl, psk = WIN_BANKS[m % 4]
                    for kb in range(KB):
                        P.add("pe", lambda e, wpar=wpar, half=half, pst=pst, psl=psl, kb=kb: e.matmul(
                            out=pst[:, psl, :],
                            lhsT=wout_sb[:, wpar, 8 * half + kb, :],
                            rhs=xn_sb[:, kb, :],
                            start=(kb == 0), stop=(kb == KB - 1)),
                            reads=[("wout", wpar), ("xn", kb)], writes=[(psk, psl)])
                    zi = m - 16
                    P.add("act", lambda e, zi=zi, pst=pst, psl=psl: e.activation(
                        out=g_sb[:, zi, :], in_=pst[:, psl, :], func=AF.Silu),
                        reads=[(psk, psl)], writes=[("g", zi)])
                    if half == 1 and p + 2 < 8:
                        emit_zpair_dma(p + 2)

            FFN_BANKS = [(ps_a, 0), (ps_a, 1), (ps_b, 0), (ps_b, 1)]
            FFN_BKEY = [("psa", 0), ("psa", 1), ("psb", 0), ("psb", 1)]

            def ffn_w1(kg, wave, ci=ci, scratch_done=scratch_done):
                for i in range(4):
                    ht = 8 * kg + 4 * wave + i
                    spar = ht % 2
                    dma("syncq", s1_sb[:, spar, :, :], s1R[ht], writes=[("s1", spar)],
                        deps=(scratch_done if ci == 0 else ()), stream="s1_%d" % spar)
                    pst, psl = FFN_BANKS[i]
                    for kb in range(KB):
                        P.add("pe", lambda e, spar=spar, kb=kb, pst=pst, psl=psl: e.matmul(
                            out=pst[:, psl, :], lhsT=s1_sb[:, spar, kb, :],
                            rhs=xn_sb[:, kb, :],
                            start=(kb == 0), stop=(kb == KB - 1)),
                            reads=[("s1", spar), ("xn", kb)], writes=[FFN_BKEY[i]])

            def ffn_gelu(kg, wave):
                for i in range(4):
                    ht = 8 * kg + 4 * wave + i
                    hh = 4 * wave + i
                    pst, psl = FFN_BANKS[i]
                    P.add("act", lambda e, ht=ht, hh=hh, pst=pst, psl=psl: e.activation(
                        out=h1grp[:, hh, :], in_=pst[:, psl, :], func=AF.Gelu,
                        scale=alpha1c[:, ht:ht + 1], bias=b1c_sb[:, ht:ht + 1]),
                        reads=[FFN_BKEY[i], ("alpha1c",), ("b1c",)], writes=[("h1", hh)])

            def ffn_w2(kg, ci=ci, scratch_done=scratch_done):
                for m in range(8):
                    ppar = (8 * kg + m) % 2
                    dma("syncq", s2_sb[:, ppar, :, :],
                        s2R[:, 8 * kg:8 * kg + 8, 128 * m:128 * (m + 1)],
                        writes=[("s2", ppar)], deps=(scratch_done if ci == 0 else ()),
                        stream="s2_%d" % ppar)
                    if True:
                        sl = m % 2
                        for k8 in range(8):
                            P.add("pe", lambda e, ppar=ppar, sl=sl, k8=k8: e.matmul(
                                out=ps_m[:, sl, :],
                                lhsT=s2_sb[:, ppar, k8, :],
                                rhs=h1grp[:, k8, :],
                                start=(k8 == 0), stop=(k8 == 7)),
                                reads=[("s2", ppar)] + [("h1", k8)], writes=[("psm", sl)])
                        ts4 = m % 2
                        if kg == 0:
                            P.add("act", lambda e, m=m, sl=sl, ts4=ts4: e.activation(
                                out=ts_buf[:, ts4, :], in_=ps_m[:, sl, :], func=AF.Identity,
                                scale=alpha2c[:, m:m + 1], bias=b2c_sb[:, m:m + 1]),
                                reads=[("psm", sl), ("alpha2c",), ("b2c",)],
                                writes=[("ts", ts4)])
                        else:
                            P.add("act", lambda e, m=m, sl=sl, ts4=ts4: e.activation(
                                out=ts_buf[:, ts4, :], in_=ps_m[:, sl, :], func=AF.Identity,
                                scale=alpha2c[:, m:m + 1]),
                                reads=[("psm", sl), ("alpha2c",)], writes=[("ts", ts4)])

                        def xadd(m=m, ts4=ts4):
                            P.add("dve", lambda e: e.tensor_tensor(
                                out=x_sb[:, m, :], in0=ts_buf[:, ts4, :], in1=x_sb[:, m, :],
                                op=OP.add),
                                reads=[("ts", ts4), ("x", m)], writes=[("x", m)])
                        pending_adds.append(xadd)
                        if len(pending_adds) >= 2:
                            pending_adds.pop(0)()

            emit_zprefetch()

            for kb16 in range(16):
                P.add("pe", lambda e, kb16=kb16: e.matmul(
                    out=ps_b[0:64, 0, :], lhsT=wx_sb[:, kb16, 0:64], rhs=xcv_sb[:, kb16, :],
                    start=(kb16 == 0), stop=(kb16 == 15)),
                    reads=[("wx",), ("xcv", kb16)], writes=[("psb", 0)])
            for kb16 in range(16):
                P.add("pe", lambda e, kb16=kb16: e.matmul(
                    out=ps_b[0:32, 1, :], lhsT=wx_sb[:, kb16, 64:96], rhs=xcv_sb[:, kb16, :],
                    start=(kb16 == 0), stop=(kb16 == 15)),
                    reads=[("wx",), ("xcv", kb16)], writes=[("psb", 1)])
            P.add("act", lambda e: e.activation(
                out=dtl_sb[:], in_=ps_b[0:64, 0, :], func=AF.Copy),
                reads=[("psb", 0)], writes=[("dtl",)])
            P.add("act", lambda e: e.activation(
                out=bc_sb[:], in_=ps_b[0:32, 1, :], func=AF.Copy),
                reads=[("psb", 1)], writes=[("bc",)])

            def emit_dtx(mdt):
                P.add("dve", lambda e, mdt=mdt: e.tensor_mul(
                    out=dtx_sb[:, mdt, :], in0=dt_sb[:, mdt, :], in1=xcv_sb[:, mdt, :]),
                    reads=[("dt", mdt), ("xcv", mdt)], writes=[("dtx", mdt)])

            for mdt in range(16):
                sl = mdt % 2
                P.add("pe", lambda e, mdt=mdt, sl=sl: e.matmul(
                    out=ps_b[:, sl, :], lhsT=wdt_sb[:, 128 * mdt:128 * (mdt + 1)],
                    rhs=dtl_sb[:], start=True, stop=True),
                    reads=[("wdt",), ("dtl",)], writes=[("psb", sl)])
                P.add("dve", lambda e, mdt=mdt, sl=sl: e.tensor_scalar(
                    out=sp_buf[:, sl, :], in0=ps_b[:, sl, :],
                    scalar1=bdt_sb[:, mdt:mdt + 1], scalar2=30.0,
                    op0=OP.add, op1=OP.min),
                    reads=[("psb", sl), ("bdt",)], writes=[("sp", sl)])
                P.add("act", lambda e, sl=sl: e.activation(
                    out=ln_tmp[:], in_=sp_buf[:, sl, :], func=AF.Exp),
                    reads=[("sp", sl)], writes=[("ln_tmp",)])
                P.add("act", lambda e, mdt=mdt, sl=sl: e.activation(
                    out=dt_sb[:, mdt, :], in_=ln_tmp[:], func=AF.Ln, bias=1.0),
                    reads=[("ln_tmp",)], writes=[("dt", mdt)])
                if mdt >= 1:
                    emit_dtx(mdt - 1)
            emit_dtx(15)

            if DEBUG and ci == 0:
                dmaD("gpq", ap["d_dt"], dt_sb[:], reads=[("dt", z) for z in range(16)])
                dmaD("gpq", ap["d_dtx"], dtx_sb[:], reads=[("dtx", z) for z in range(16)])
                dmaD("gpq", ap["d_bc"], bc_sb[:], reads=[("bc",)])
                dmaD("gpq", ap["d_dtl"], dtl_sb[:], reads=[("dtl",)])

            rp = ci % 2
            wp = 1 - rp
            pending_yacc = None
            pending_adds = []
            if seq_start:
                P.add("dve", lambda e, rp=rp: e.memset(h_state[:, rp, :], 0.0),
                      reads=[], writes=[("hstate", rp)])
                if DEBUG and ci == 0:
                    P.add("dve", lambda e: e.memset(h_guard1[:], 777.0), writes=[("hg1",)])
                    P.add("dve", lambda e: e.memset(h_guard2[:], 777.0), writes=[("hg2",)])
            j = 0

            def emit_bcast(n):
                bs = n % 2
                P.add("pe", lambda e, n=n: e.matmul(
                    out=ps_bc[:, 0, :], lhsT=E_sb[:, 128 * n:128 * (n + 1)],
                    rhs=bc_sb[:], start=True, stop=True),
                    reads=[("E_sb",), ("bc",)], writes=[("psbc", 0)])
                P.add("act", lambda e, bs=bs: e.activation(
                    out=Bbc_sb[:, bs, :], in_=ps_bc[:, 0, :], func=AF.Copy),
                    reads=[("psbc", 0)], writes=[("Bbc", bs)])
                P.add("pe", lambda e, n=n: e.matmul(
                    out=ps_bc[:, 1, :], lhsT=E_sb[:, 128 * (16 + n):128 * (17 + n)],
                    rhs=bc_sb[:], start=True, stop=True),
                    reads=[("E_sb",), ("bc",)], writes=[("psbc", 1)])
                P.add("act", lambda e, bs=bs: e.activation(
                    out=Cbc_sb[:, bs, :], in_=ps_bc[:, 1, :], func=AF.Copy),
                    reads=[("psbc", 1)], writes=[("Cbc", bs)])

            def emit_dA(n, db0, jas):
                for i in range(4):
                    db = db0 + i
                    col = db * 16 + n
                    ja = jas[i]
                    P.add("act", lambda e, db=db, col=col, ja=ja: e.activation(
                        out=dA_buf[:, ja, :], in_=dt_sb[:, db, :], func=AF.Exp,
                        scale=A_sb[:, col:col + 1]),
                        reads=[("dt", db), ("A_sb",)], writes=[("dA", ja)])

            emit_bcast(0)
            emit_bcast(1)
            emit_dA(0, 0, [0, 1, 2, 3])
            emit_zhalf()
            for n in range(NST):
                npar = n % 2
                for dbq in range(DBL // 4):
                    db0 = 4 * dbq
                    hq = dbq % 2
                    jas = [(j + i) % 4 for i in range(4)]
                    if not (n == 0 and dbq == 0):
                        emit_dA(n, db0, jas)
                    P.add("dve", lambda e, db0=db0, bs=n % 2: e.tensor_tensor(
                        out=dBx_buf[:], in0=dtx_sb[:, db0:db0 + 4, :],
                        in1=Bbc_sb[:, bs, None, :].broadcast_to([128, 4, T]),
                        op=OP.mult),
                        reads=[("dtx", db0), ("Bbc", n % 2)],
                        writes=[("dBx",)])
                    for i in range(4):
                        col = (db0 + i) * 16 + n
                        ja = jas[i]
                        P.add("dve", lambda e, col=col, ja=ja, hs=4 * hq + i, i=i, rp=rp: e.tensor_tensor_scan(
                            out=H_buf[:, hs, :], data0=dA_buf[:, ja, :],
                            data1=dBx_buf[:, i, :],
                            initial=h_state[:, rp, col:col + 1], op0=OP.mult, op1=OP.add),
                            reads=[("dA", ja), ("dBx",)], writes=[("H", 4 * hq + i), ("hstate", rp)])
                    P.add("dve", lambda e, hq=hq, bs=n % 2: e.tensor_tensor(
                        out=HC_buf[:], in0=H_buf[:, 4 * hq:4 * hq + 4, :],
                        in1=Cbc_sb[:, bs, None, :].broadcast_to([128, 4, T]),
                        op=OP.mult),
                        reads=[("H", 4 * hq), ("H", 4 * hq + 1), ("H", 4 * hq + 2),
                               ("H", 4 * hq + 3), ("Cbc", n % 2)],
                        writes=[("HC",)])
                    if n == 0:
                        P.add("dve", lambda e, db0=db0: e.tensor_copy(
                            out=y_sb[:, db0:db0 + 4, :], in_=HC_buf[:]),
                            reads=[("HC",)],
                            writes=[("y", db0), ("y", db0 + 1), ("y", db0 + 2), ("y", db0 + 3)])
                    else:
                        P.add("dve", lambda e, db0=db0: e.tensor_tensor(
                            out=y_sb[:, db0:db0 + 4, :], in0=HC_buf[:],
                            in1=y_sb[:, db0:db0 + 4, :], op=OP.add),
                            reads=[("HC",), ("y", db0), ("y", db0 + 1), ("y", db0 + 2), ("y", db0 + 3)],
                            writes=[("y", db0), ("y", db0 + 1), ("y", db0 + 2), ("y", db0 + 3)])

                    def yacc_state(db0=db0, hq=hq, wp=wp, n=n):
                        for i in range(4):
                            col = (db0 + i) * 16 + n
                            P.add("act", lambda e, col=col, hs=4 * hq + i: e.copy(
                                out=h_state[:, wp, col:col + 1], in_=H_buf[:, hs, T - 1:T]),
                                reads=[("H", 4 * hq + i)], writes=[("hstate", wp)])
                            if DEBUG and ci == 0:
                                P.add("act", lambda e, col=col, hs=4 * hq + i: e.copy(
                                    out=hdbg_sb[:, col:col + 1], in_=H_buf[:, hs, T - 1:T]),
                                    reads=[("H", 4 * hq + i)], writes=[("hdbg", col)])
                    if pending_yacc is not None:
                        pending_yacc()
                    pending_yacc = yacc_state
                    if pending_adds:
                        pending_adds.pop(0)()
                    j += 4
                if n + 2 < NST:
                    emit_bcast(n + 2)
                if n % 4 == 0:
                    ffn_w1(n // 4, 0)
                elif n % 4 == 1:
                    ffn_gelu(n // 4, 0)
                    ffn_w1(n // 4, 1)
                elif n % 4 == 2:
                    ffn_gelu(n // 4, 1)
                else:
                    ffn_w2(n // 4)

            if DEBUG and ci == 0:
                dmaD("gpq", ap["d_y"], y_sb[:], reads=[("y", z) for z in range(16)])
                dmaD("gpq", ap["d_hstate"], h_state[:, 1, :], reads=[("hstate", 1)])
                dmaD("gpq", ap["d_hcols"], hdbg_sb[:], reads=[("hdbg", c) for c in range(256)])
                P.add("dve", lambda e: e.tensor_copy(out=hdbg_sb[:], in_=h_state[:, 1, :]),
                      reads=[("hstate", 1)],
                      writes=[("hsnap",)] + [("hdbg", c) for c in range(256)])
                dmaD("gpq", ap["d_hsnap"], hdbg_sb[:], reads=[("hsnap",)])

            if pending_yacc is not None:
                pending_yacc()
                pending_yacc = None
            for a in pending_adds:
                a()
            pending_adds = []

            for db in range(DBL):
                dp = db % 2
                P.add("dve", lambda e, db=db, dp=dp: e.scalar_tensor_tensor(
                    out=HC_buf[:, dp, :], in0=xcv_sb[:, db, :],
                    scalar=Dp_sb[:, db:db + 1], in1=y_sb[:, db, :],
                    op0=OP.mult, op1=OP.add),
                    reads=[("xcv", db), ("y", db), ("Dp",)], writes=[("HC", dp)])
                P.add("dve", lambda e, db=db, dp=dp: e.tensor_mul(
                    out=y_sb[:, db, :], in0=HC_buf[:, dp, :], in1=g_sb[:, db, :]),
                    reads=[("HC", dp), ("g", db)], writes=[("y", db)])

            if DEBUG and ci == 0:
                dmaD("gpq", ap["d_ymix"], y_sb[:], reads=[("y", z) for z in range(16)])

            if DEBUG:
                dmaD("gpq", ap["d_ymixA"][ci], y_sb[:], reads=[("y", z) for z in range(16)])

            for m in range(8):
                wpar = m % 2
                dma("syncq", wout_sb[:, wpar, :, :], WoutR[m],
                    reads=[("woutB", m)], writes=[("wout", wpar)],
                    stream="wout%d" % wpar)
                for k16 in range(16):
                    P.add("pe", lambda e, m=m, wpar=wpar, k16=k16: e.matmul(
                        out=ps_m[:, m % 2, :], lhsT=wout_sb[:, wpar, k16, :],
                        rhs=y_sb[:, k16, :], start=(k16 == 0), stop=(k16 == 15)),
                        reads=[("wout", wpar), ("y", k16)], writes=[("psm", m % 2)])
                P.add("dve", lambda e, m=m: e.tensor_tensor(
                    out=x_sb[:, m, :], in0=ps_m[:, m % 2, :], in1=x_sb[:, m, :], op=OP.add),
                    reads=[("psm", m % 2), ("x", m)], writes=[("x", m)])

            if DEBUG:
                dmaD("gpq", ap["d_xfinA"][ci], x_sb[:], reads=[("x", m) for m in range(8)])

            for m in range(8):
                dma("syncq", OcR[ci][:, m, :], x_sb[:, m, :],
                    reads=[("x", m)], writes=[("store", ci, m)], stream="store")
            if DEBUG and ci == NCH - 1:
                P.add("dve", lambda e: e.tensor_copy(out=ln_tmp[:, 0:8], in_=h_guard1[:]),
                      reads=[("hg1",)], writes=[("gcopy", 0)])
                P.add("dve", lambda e: e.tensor_copy(out=ln_tmp[:, 8:16], in_=h_guard2[:]),
                      reads=[("hg2",)], writes=[("gcopy", 1)])
                dmaD("gpq", ap["d_guards"], ln_tmp[:, 0:128],
                    reads=[("gcopy", 0), ("gcopy", 1)])

        totals = plan.finalize()

        sems = {}
        sem_ctx = contextlib.ExitStack()
        es.enter_context(sem_ctx)
        for e in ("pe", "act", "dve", "pool"):
            if totals.get(e):
                sems[e] = sem_ctx.enter_context(nc.semaphore("sem_" + e))
        if "pool" not in sems and any(o.eng == "pool" for o in plan.ops):
            sems["pool"] = sem_ctx.enter_context(nc.semaphore("sem_pool"))
        for s in plan.stream_totals:
            sems[s] = sem_ctx.enter_context(nc.semaphore("sem_" + s))

        def sem_of(op):
            return op.stream if op.eng in ("syncq", "gpq") else op.eng

        block = es.enter_context(nc.Block())

        GROUPS = {"pe": ("pe",), "act": ("act",), "dve": ("dve",),
                  "syncq": ("syncq",), "gpq": ("gpq", "pool")}

        def emit(engname):
            members = GROUPS[engname]

            def skip_dep(op, d):
                if d.eng not in members:
                    return False
                if d.eng not in ("syncq", "gpq") and d.eng == op.eng:
                    return True
                if d.eng == "pool" and op.eng == "gpq":
                    return True
                if d.eng == "gpq" and op.eng == "pool":
                    return False
                if d.eng in ("syncq", "gpq") and op.eng in ("syncq", "gpq"):
                    if sem_of(d) == getattr(op, "stream", None):
                        return True
                    if {getattr(op, "stream", None), sem_of(d)} == {"xload", "store"}:
                        return True
                return False

            def run(engine):
                watermark = {}
                for op in plan.ops:
                    if op.eng not in members:
                        continue
                    needed = {}
                    for d in op.deps:
                        if skip_dep(op, d):
                            continue
                        s = sem_of(d)
                        assert d.val is not None, (d.eng, s)
                        if d.val > needed.get(s, 0):
                            needed[s] = d.val
                    for s, v in sorted(needed.items()):
                        if v > watermark.get(s, 0):
                            engine.wait_ge(sems[s], v)
                            watermark[s] = v
                    inst = op.fn(engine)
                    if op.need:
                        inst.then_inc(sems[sem_of(op)], op.amt)
                if engname == "syncq":
                    engine.wait_ge(sems["store"], plan.stream_totals["store"])
                if engname == "gpq" and "dump" in plan.stream_totals:
                    engine.wait_ge(sems["dump"], plan.stream_totals["dump"])
            return run

        block.tensor(emit("pe"))
        block.scalar(emit("act"))
        block.vector(emit("dve"))
        block.sync(emit("syncq"))
        block.gpsimd(emit("gpq"))

    return nc


def kernel(L0_attn_x, L0_attn_W_in, L0_attn_conv_w, L0_attn_conv_b, L0_attn_W_x,
           L0_attn_W_dt, L0_attn_b_dt, L0_attn_A_log, L0_attn_Dp, L0_attn_W_out,
           L0_ffn_w1, L0_ffn_b1, L0_ffn_w2, L0_ffn_b2):
    f = np.float32
    asc = np.ascontiguousarray

    WinS = asc(L0_attn_W_in.T.reshape(1024, 32, 128).transpose(1, 0, 2), f)
    w1T = asc(L0_ffn_w1.T, f)
    w2T = asc(L0_ffn_w2.T, f)
    w1o = asc(L0_ffn_w1, f)
    w2o = asc(L0_ffn_w2, f)
    WoutS = asc(L0_attn_W_out.T.reshape(2048, 8, 128).transpose(1, 0, 2), f)
    WxTr = asc(L0_attn_W_x.T.reshape(16, 128, 96).transpose(1, 0, 2), f)
    WdtT = asc(L0_attn_W_dt.T, f)
    Ar = asc(L0_attn_A_log.reshape(16, 128, 16).transpose(1, 0, 2).reshape(128, 256), f)
    convw = asc(L0_attn_conv_w.reshape(16, 128, 4).transpose(1, 0, 2).reshape(128, 64), f)
    convb = asc(L0_attn_conv_b.reshape(16, 128).T, f)
    bdt = asc(L0_attn_b_dt.reshape(16, 128).T, f)
    Dpr = asc(L0_attn_Dp.reshape(16, 128).T, f)
    b1r = asc(L0_ffn_b1.reshape(32, 128).T, f)
    b2c = asc(L0_ffn_b2.reshape(8, 128).T, f)
    eye = asc(np.kron(np.eye(32, dtype=f), np.ones((1, 128), f)))
    ones128 = np.ones((128, 1), f)
    ones1x128 = np.ones((1, 128), f)

    shared = dict(WinS=WinS, w1T=w1T, w2T=w2T, w1o=w1o, w2o=w2o, WoutS=WoutS,
                  WxTr=WxTr, WdtT=WdtT, Ar=Ar, convw=convw, convb=convb, bdt=bdt,
                  Dpr=Dpr, b1r=b1r, b2c=b2c, eye=eye, ones128=ones128,
                  ones1x128=ones1x128)

    in_maps = []
    x = np.asarray(L0_attn_x, f)
    for i in range(NCORES):
        xi = x[2 * i:2 * i + 2]
        Xc = asc(xi.reshape(2, 4, T, 1024).transpose(0, 1, 3, 2).reshape(NCH, 1024, T))
        m = dict(shared)
        m["Xc"] = Xc
        in_maps.append(m)

    nc = build_nc()
    res = run_bass_kernel_spmd(nc, in_maps, core_ids=list(range(NCORES)))

    out = np.empty((16, 2048, 1024), f)
    for i in range(NCORES):
        Oc = res.results[i]["Oc"]
        out[2 * i:2 * i + 2] = (
            Oc.reshape(2, 4, 1024, T).transpose(0, 1, 3, 2).reshape(2, 2048, 1024))
    return out


if __name__ == "__main__":
    import sys
    if "build" in sys.argv:
        nc = build_nc()
        print("build OK")


# version 48
# speedup vs baseline: 1.0264x; 1.0264x over previous
import contextlib
import numpy as np
import ml_dtypes
import concourse.bass as bass
import concourse.mybir as mybir
from concourse.bass_utils import run_bass_kernel_spmd

F32, BF16 = mybir.dt.float32, mybir.dt.bfloat16
AF, OP = mybir.ActivationFunctionType, mybir.AluOpType
AX = mybir.AxisListType

NCORES = 8
BSEQ = 2
L = 2048
T = 512
NCH = BSEQ * (L // T)
D = 1024
DI = 2048
HF = 4096
NST = 16
KB = D // 128
DBL = DI // 128
HT = HF // 128
DEBUG = False



class _Op:
    __slots__ = ("eng", "fn", "deps", "val", "need", "amt", "stream")
    def __init__(self, eng, fn, deps, amt):
        self.eng, self.fn, self.deps, self.amt = eng, fn, deps, amt
        self.val = None
        self.need = False


class Plan:
    ENGS = ("pe", "act", "dve", "pool", "syncq", "gpq")
    TOTAL_STREAMS = ("const_s", "const_g")

    def __init__(self):
        self.ops = []
        self.track = {}

    def add(self, eng, fn, reads=(), writes=(), deps=(), stream=None):
        dl = list(deps)
        for k in reads:
            w, _ = self.track.get(k, (None, []))
            if w is not None:
                dl.append(w)
        for k in writes:
            w, rs = self.track.get(k, (None, []))
            if w is not None:
                dl.append(w)
            dl.extend(rs)
        amt = 16 if eng in ("syncq", "gpq") else 1
        op = _Op(eng, fn, dl, amt)
        op.stream = stream
        if eng in ("syncq", "gpq"):
            assert stream is not None, "DMA ops need a stream label"
        self.ops.append(op)
        for k in reads:
            w, rs = self.track.get(k, (None, []))
            rs.append(op)
            self.track[k] = (w, rs)
        for k in writes:
            self.track[k] = (op, [])
        return op

    def finalize(self):
        for op in self.ops:
            if op.eng in ("syncq", "gpq"):
                op.need = True
            for d in op.deps:
                if d.eng != op.eng:
                    d.need = True
        cnt = {}
        scnt = {}
        for op in self.ops:
            if op.eng in ("syncq", "gpq"):
                scnt[op.stream] = scnt.get(op.stream, 0) + 16
                op.val = scnt[op.stream]
            elif op.need:
                cnt[op.eng] = cnt.get(op.eng, 0) + 1
                op.val = cnt[op.eng]
        for op in self.ops:
            if op.eng in ("syncq", "gpq") and op.stream in self.TOTAL_STREAMS:
                op.val = scnt[op.stream]
        self.stream_totals = scnt
        self.totals = cnt
        return cnt


def build_nc():
    nc = bass.Bass("TRN2", target_bir_lowering=False, debug=False)
    ap = {}

    def din(name, shape):
        ap[name] = nc.dram_tensor(name, list(shape), F32, kind="ExternalInput").ap()

    din("Xc", (NCH, 1024, T))
    din("WinS", (32, 1024, 128))
    din("w1T", (1024, 4096))
    din("w2T", (4096, 1024))
    din("w1o", (4096, 1024))
    din("w2o", (1024, 4096))
    din("WoutS", (8, 2048, 128))
    din("WxTr", (128, 16, 96))
    din("WdtT", (64, 2048))
    din("Ar", (128, 256))
    din("convw", (128, 64))
    din("convb", (128, 16))
    din("bdt", (128, 16))
    din("Dpr", (128, 16))
    din("b1r", (128, 32))
    din("b2c", (128, 8))
    din("eye", (32, 4096))
    din("ones128", (128, 1))
    din("ones1x128", (1, 128))
    ap["Oc"] = nc.dram_tensor("Oc", [NCH, 1024, T], F32, kind="ExternalOutput").ap()
    if DEBUG:
        for nm, shp in [("d_xn", (128, 8, T)), ("d_g", (128, 16, T)), ("d_xcv", (128, 16, T)),
                        ("d_dt", (128, 16, T)), ("d_dtx", (128, 16, T)), ("d_y", (128, 16, T)),
                        ("d_ymix", (128, 16, T)), ("d_xffn", (128, 8, T)), ("d_rows", (1, 8, T)),
                        ("d_bc", (32, T)), ("d_dtl", (64, T)), ("d_alpha1", (128, 32)),
                        ("d_alpha2", (128, 8)), ("d_A", (128, 256)), ("d_h1", (128, 8, T)),
                        ("d_hstate", (128, 256)), ("d_hcols", (128, 256)), ("d_guards", (128, 128)), ("d_hsnap", (128, 256)),
                        ("d_xffnA", (NCH, 128, 8, T)), ("d_xfinA", (NCH, 128, 8, T)), ("d_ymixA", (NCH, 128, 16, T)),
                        ("d_ffnp", (NCH, 4, 128, T)), ("d_h1A", (NCH, 4, 128, 8, T))]:
            ap[nm] = nc.dram_tensor(nm, list(shp), F32, kind="ExternalOutput").ap()
    ap["s1S"] = nc.dram_tensor("s1S", [32, 1024, 128], BF16, kind="Internal").ap()
    ap["winB"] = nc.dram_tensor("winB", [32, 1024, 128], BF16, kind="Internal").ap()
    ap["woutB"] = nc.dram_tensor("woutB", [8, 2048, 128], BF16, kind="Internal").ap()
    ap["s2S"] = nc.dram_tensor("s2S", [4096, 1024], BF16, kind="Internal").ap()

    es = contextlib.ExitStack()

    def sb(name, shape, dt):
        return es.enter_context(nc.sbuf_tensor(name, list(shape), dt))

    def psum(name, shape):
        return es.enter_context(nc.psum_tensor(name, list(shape), F32))

    plan = Plan()

    with es:
        ones128_bf = sb("ones128_bf", (128, 1), BF16)
        ones1x128_f = sb("ones1x128_f", (1, 128), F32)
        E_sb = sb("E_sb", (32, 4096), BF16)
        convw_sb = sb("convw_sb", (128, 64), F32)
        convb_sb = sb("convb_sb", (128, 16), F32)
        bdt_sb = sb("bdt_sb", (128, 16), F32)
        Dp_sb = sb("Dp_sb", (128, 16), F32)
        b1c_sb = sb("b1c_sb", (128, 32), F32)
        b2c_sb = sb("b2c_sb", (128, 8), F32)
        alpha1c = sb("alpha1c", (128, 32), F32)
        alpha2c = sb("alpha2c", (128, 8), F32)
        alpha1r = sb("alpha1r", (128, 32), F32)
        alpha2r = sb("alpha2r", (128, 8), F32)
        wx_sb = sb("wx_sb", (128, 16, 96), BF16)
        wdt_sb = sb("wdt_sb", (64, 2048), BF16)
        A_load = sb("A_load", (128, 256), F32)
        A_sb = sb("A_sb", (128, 256), F32)

        P = plan

        def dma(eng, out_ap, in_ap, reads=(), writes=(), deps=(), stream=None):
            if stream is None:
                stream = "const_s" if eng == "syncq" else "const_g"
            return P.add(eng, lambda e, o=out_ap, i=in_ap: e.dma_start(out=o, in_=i),
                         reads=reads, writes=writes, deps=deps, stream=stream)

        def dmaD(eng, out_ap, in_ap, reads=(), writes=(), deps=()):
            return dma(eng, out_ap, in_ap, reads=reads, writes=writes, deps=deps,
                       stream="dump")

        dma("gpq", ones128_bf[:], ap["ones128"], writes=[("ones128_bf",)])
        dma("syncq", ones1x128_f[:], ap["ones1x128"], writes=[("ones1x128_f",)])
        dma("gpq", E_sb[:], ap["eye"], writes=[("E_sb",)])
        dma("syncq", convw_sb[:], ap["convw"], writes=[("convw",)])
        dma("syncq", convb_sb[:], ap["convb"], writes=[("convb",)])
        dma("syncq", bdt_sb[:], ap["bdt"], writes=[("bdt",)])
        dma("syncq", Dp_sb[:], ap["Dpr"], writes=[("Dp",)])
        dma("syncq", b1c_sb[:], ap["b1r"], writes=[("b1c",)])
        dma("syncq", b2c_sb[:], ap["b2c"], writes=[("b2c",)])
        dma("gpq", wx_sb[:], ap["WxTr"], writes=[("wx",)])
        dma("gpq", wdt_sb[:], ap["WdtT"], writes=[("wdt",)])
        dma("syncq", A_load[:], ap["Ar"], writes=[("A_load",)])

        P.add("act", lambda e: e.activation(out=A_sb[:], in_=A_load[:], func=AF.Exp),
              reads=[("A_load",)], writes=[("A_sb",)])
        P.add("dve", lambda e: e.tensor_scalar_mul(out=A_sb[:], in0=A_sb[:], scalar1=-1.0),
              reads=[("A_sb",)], writes=[("A_sb",)])

        WinR_ = ap["WinS"].rearrange("s (kb p) m -> s p kb m", p=128)
        WoutR_ = ap["WoutS"].rearrange("s (k p) mm -> s p k mm", p=128)

        x_sb = sb("x_sb", (128, 8, T), F32)
        xn_sb = sb("xn_sb", (128, 8, T), BF16)
        xb_t = sb("xb_t", (128, 2, T), BF16)
        x2b_t = sb("x2b_t", (128, 2, T), BF16)
        rowsA = sb("rowsA", (128, T), F32)
        rowsB = sb("rowsB", (128, T), F32)
        rowsC = sb("rowsC", (128, T), F32)
        _rowmap = [(0, 0), (0, 32), (1, 32), (2, 32), (0, 64), (1, 0), (2, 0)]

        def rowap(i):
            t, p = _rowmap[i]
            buf = (rowsA, rowsB, rowsC)[t]
            return buf[p:p + 1, :]
        ln_tmp = sb("ln_tmp", (128, T), F32)
        ps_a = psum("ps_a", (128, 2, T))
        ps_b = psum("ps_b", (128, 2, T))
        ps_m = psum("ps_m", (128, 2, T))
        ps_bc = psum("ps_bc", (128, 2, T))

        XcR = ap["Xc"].rearrange("c (kb p) f -> c p kb f", p=128)

        def emit_ln(ci, tail0=None):
            for kb in range(KB):
                par = kb % 2
                P.add("dve", lambda e, kb=kb, par=par: e.tensor_copy(
                    out=xb_t[:, par, :], in_=x_sb[:, kb, :]),
                    reads=[("x", kb)], writes=[("xb_t", par)])
                P.add("act", lambda e, kb=kb, par=par: e.activation(
                    out=x2b_t[:, par, :], in_=x_sb[:, kb, :], func=AF.Square),
                    reads=[("x", kb)], writes=[("x2b_t", par)])
                P.add("pe", lambda e, kb=kb, par=par: e.matmul(
                    out=ps_m[0:1, 0, :], lhsT=ones128_bf[:], rhs=xb_t[:, par, :],
                    start=(kb == 0), stop=(kb == KB - 1)),
                    reads=[("xb_t", par), ("ones128_bf",)], writes=[("psm", 0)])
                P.add("pe", lambda e, kb=kb, par=par: e.matmul(
                    out=ps_m[0:1, 1, :], lhsT=ones128_bf[:], rhs=x2b_t[:, par, :],
                    start=(kb == 0), stop=(kb == KB - 1)),
                    reads=[("x2b_t", par), ("ones128_bf",)], writes=[("psm", 1)])
            P.add("dve", lambda e: e.tensor_scalar_mul(
                out=rowap(0), in0=ps_m[0:1, 0, :], scalar1=1.0 / D),
                reads=[("psm", 0)], writes=[("row", 0)])
            P.add("dve", lambda e: e.tensor_scalar_mul(
                out=rowap(1), in0=ps_m[0:1, 1, :], scalar1=1.0 / D),
                reads=[("psm", 1)], writes=[("row", 1)])
            P.add("act", lambda e: e.activation(
                out=rowap(2), in_=rowap(0), func=AF.Square),
                reads=[("row", 0)], writes=[("row", 2)])
            P.add("dve", lambda e: e.scalar_tensor_tensor(
                out=rowap(3), in0=rowap(1), scalar=1e-5,
                in1=rowap(2), op0=OP.add, op1=OP.subtract),
                reads=[("row", 1), ("row", 2)], writes=[("row", 3)])
            P.add("act", lambda e: e.activation(
                out=rowap(4), in_=rowap(3), func=AF.Ln),
                reads=[("row", 3)], writes=[("row", 4)])
            P.add("act", lambda e: e.activation(
                out=rowap(5), in_=rowap(4), func=AF.Exp, scale=-0.5),
                reads=[("row", 4)], writes=[("row", 5)])
            P.add("dve", lambda e: e.tensor_mul(
                out=rowap(6), in0=rowap(0), in1=rowap(5)),
                reads=[("row", 0), ("row", 5)], writes=[("row", 6)])
            P.add("pe", lambda e: e.matmul(
                out=ps_m[:, 0, :], lhsT=ones1x128_f[:], rhs=rowap(5),
                start=True, stop=True),
                reads=[("row", 5), ("ones1x128_f",)], writes=[("psm", 0)])
            P.add("pe", lambda e: e.matmul(
                out=ps_m[:, 1, :], lhsT=ones1x128_f[:], rhs=rowap(6),
                start=True, stop=True),
                reads=[("row", 6), ("ones1x128_f",)], writes=[("psm", 1)])
            for kb in range(KB):
                P.add("dve", lambda e, kb=kb: e.tensor_mul(
                    out=ln_tmp[:], in0=x_sb[:, kb, :], in1=ps_m[:, 0, :]),
                    reads=[("x", kb), ("psm", 0)], writes=[("ln_tmp",)])
                P.add("dve", lambda e, kb=kb: e.tensor_sub(
                    out=xn_sb[:, kb, :], in0=ln_tmp[:], in1=ps_m[:, 1, :]),
                    reads=[("ln_tmp",), ("psm", 1)], writes=[("xn", kb)])
                if kb == KB - 2 and tail0 is not None:
                    tail0()

        dma("syncq", x_sb[:], XcR[0], writes=[("x", kb) for kb in range(KB)],
            stream="xload")
        emit_ln(0)

        with contextlib.ExitStack() as prep_es:
            ptile_f = prep_es.enter_context(nc.sbuf_tensor("ptile_f", [128, 2, 4096], F32))
            ptile_s = prep_es.enter_context(nc.sbuf_tensor("ptile_s", [128, 2, 4096], BF16))

            winB_w = ap["winB"].rearrange("s (kb p) mm -> s p kb mm", p=128)
            woutB_w = ap["woutB"].rearrange("s (k p) mm -> s p k mm", p=128)
            for m in range(32):
                par = m % 2
                view = ptile_s[:, par, 0:1024].rearrange("p (kb mm) -> p kb mm", mm=128)
                dma("gpq", view, WinR_[m], writes=[("ptile_s", par)],
                    stream="pwc%d" % par)
                dma("syncq", winB_w[m], view, reads=[("ptile_s", par)],
                    writes=[("winB", m)], stream="pws%d" % par)
            for m in range(8):
                par = m % 2
                view = ptile_s[:, par, 0:2048].rearrange("p (k mm) -> p k mm", mm=128)
                dma("gpq", view, WoutR_[m], writes=[("ptile_s", par)],
                    stream="pwc%d" % par)
                dma("syncq", woutB_w[m], view, reads=[("ptile_s", par)],
                    writes=[("woutB", m)], stream="pws%d" % par)

            s1S_w = ap["s1S"].rearrange("ht (kb p) mm -> kb p ht mm", p=128)
            for kbp in range(8):
                par = kbp % 2
                dma("syncq", ptile_f[:, par, :], ap["w1T"][128 * kbp:128 * (kbp + 1), :],
                    writes=[("ptile_f", par)], stream="pf%d" % par)
                P.add("act", lambda e, par=par: e.activation(
                    out=ptile_s[:, par, :], in_=ptile_f[:, par, :], func=AF.Sign),
                    reads=[("ptile_f", par)], writes=[("ptile_s", par)])
                dma("gpq", s1S_w[kbp],
                    ptile_s[:, par, :].rearrange("p (ht mm) -> p ht mm", mm=128),
                    reads=[("ptile_s", par)], writes=[("s1S", kbp)], stream="prep_out%d" % par)
            for bb in range(32):
                par = bb % 2
                dma("syncq", ptile_f[:, par, 0:1024], ap["w2T"][128 * bb:128 * (bb + 1), :],
                    writes=[("ptile_f", par)], stream="pf%d" % par)
                P.add("act", lambda e, par=par: e.activation(
                    out=ptile_s[:, par, 0:1024], in_=ptile_f[:, par, 0:1024], func=AF.Sign),
                    reads=[("ptile_f", par)], writes=[("ptile_s", par)])
                dma("gpq", ap["s2S"][128 * bb:128 * (bb + 1), :], ptile_s[:, par, 0:1024],
                    reads=[("ptile_s", par)], writes=[("s2S", bb)], stream="prep_out%d" % par)
            for ht in range(32):
                par = ht % 2
                dma("syncq", ptile_f[:, par, 0:1024], ap["w1o"][128 * ht:128 * (ht + 1), :],
                    writes=[("ptile_f", par)], stream="pf%d" % par)
                P.add("dve", lambda e, ht=ht, par=par: e.reduce_sum(
                    out=alpha1r[:, ht:ht + 1], in_=ptile_f[:, par, 0:1024], axis=AX.X,
                    apply_absolute_value=True),
                    reads=[("ptile_f", par)], writes=[("alpha1r", ht)])
            for m in range(8):
                par = m % 2
                dma("syncq", ptile_f[:, par, :], ap["w2o"][128 * m:128 * (m + 1), :],
                    writes=[("ptile_f", par)], stream="pf%d" % par)
                P.add("dve", lambda e, m=m, par=par: e.reduce_sum(
                    out=alpha2r[:, m:m + 1], in_=ptile_f[:, par, :], axis=AX.X,
                    apply_absolute_value=True),
                    reads=[("ptile_f", par)], writes=[("alpha2r", m)])
            P.add("dve", lambda e: e.tensor_scalar_mul(
                out=alpha1c[:], in0=alpha1r[:], scalar1=1.0 / 1024.0),
                reads=[("alpha1r", ht) for ht in range(32)], writes=[("alpha1c",)])
            P.add("dve", lambda e: e.tensor_scalar_mul(
                out=alpha2c[:], in0=alpha2r[:], scalar1=1.0 / 4096.0),
                reads=[("alpha2r", m) for m in range(8)], writes=[("alpha2c",)])
            last_po = {}
            for _o in P.ops:
                if getattr(_o, "stream", None) in ("prep_out0", "prep_out1",
                                                   "pws0", "pws1"):
                    last_po[_o.stream] = _o
            scratch_done = list(last_po.values())
            prep_tail = [P.ops[-1]]
            last_per_eng = {}
            for op in P.ops:
                last_per_eng[op.eng] = op
            prep_barrier = list(last_per_eng.values())

        xc_buf = sb("xc_buf", (128, 2, T + 3), BF16)
        xtail = sb("xtail", (128, 16, 3), BF16)
        sp_buf = sb("sp_buf", (128, 2, T), F32)
        conv_acc = sb("conv_acc", (128, 2, T), BF16)
        g_sb = sb("g_sb", (128, 16, T), BF16)
        xcv_sb = sb("xcv_sb", (128, 16, T), BF16)
        dt_sb = sb("dt_sb", (128, 16, T), BF16)
        dtx_sb = sb("dtx_sb", (128, 16, T), BF16)
        dtl_sb = sb("dtl_sb", (64, T), BF16)
        bc_sb = sb("bc_sb", (32, T), BF16)
        Bbc_sb = sb("Bbc_sb", (128, 1, T), BF16)
        Cbc_sb = sb("Cbc_sb", (128, 1, T), BF16)
        dA_buf = sb("dA_buf", (128, 4, T), F32)
        dBx_buf = sb("dBx_buf", (128, 4, T), BF16)
        H_buf = sb("H_buf", (128, 8, T), BF16)
        HC_buf = sb("HC_buf", (128, 4, T), BF16)
        y_sb = sb("y_sb", (128, 16, T), BF16)
        h_guard1 = sb("h_guard1", (128, 8), F32) if DEBUG else None
        h_state = sb("h_state", (128, 2, 256), F32)
        h_guard2 = sb("h_guard2", (128, 8), F32) if DEBUG else None
        h1grp = sb("h1grp", (128, 8, T), BF16)
        ts_buf = sb("ts_buf", (128, 2, T), F32)
        win_sb = sb("win_sb", (128, 2, 8, 128), BF16)
        s1_sb = sb("s1_sb", (128, 2, 8, 128), BF16)
        s2_sb = sb("s2_sb", (128, 2, 8, 128), BF16)
        wout_sb = sb("wout_sb", (128, 2, 16, 128), BF16)
        hdbg_sb = sb("hdbg_sb", (128, 256), F32) if DEBUG else None

        OcR = ap["Oc"].rearrange("c (kb p) f -> c p kb f", p=128)
        WinR = ap["winB"].rearrange("s (kb p) m -> s p kb m", p=128)
        s1R = ap["s1S"].rearrange("ht (kb p) mm -> ht p kb mm", p=128)
        s2R = ap["s2S"].rearrange("(b p) m -> p b m", p=128)
        WoutR = ap["woutB"].rearrange("s (k p) mm -> s p k mm", p=128)

        barrier_deps = prep_barrier

        for ci in range(NCH):
            seq_start = (ci % (L // T)) == 0

            def emit_tailin(db, seq_start=seq_start):
                if seq_start:
                    P.add("dve", lambda e, db=db: e.memset(xc_buf[:, db % 2, 0:3], 0.0),
                          writes=[("xcpad", db % 2)])
                else:
                    P.add("dve", lambda e, db=db: e.tensor_copy(
                        out=xc_buf[:, db % 2, 0:3], in_=xtail[:, db, :]),
                        reads=[("xtail", db)], writes=[("xcpad", db % 2)])

            if ci > 0:
                emit_ln(ci, tail0=lambda: emit_tailin(0))
            else:
                emit_tailin(0)

            WIN_BANKS = [(ps_a, 0, "psa"), (ps_a, 1, "psa"),
                         (ps_m, 0, "psm"), (ps_m, 1, "psm")]
            for m in range(16):
                wpar = m % 2
                dma("syncq", win_sb[:, wpar, :, :], WinR[m],
                    reads=[("winB", m)], writes=[("win", wpar)],
                    stream="win%d" % wpar)
                pst, psl, psk = WIN_BANKS[m % 4]
                if True:
                    for kb in range(KB):
                        P.add("pe", lambda e, wpar=wpar, pst=pst, psl=psl, kb=kb: e.matmul(
                            out=pst[:, psl, :],
                            lhsT=win_sb[:, wpar, kb, :],
                            rhs=xn_sb[:, kb, :],
                            start=(kb == 0), stop=(kb == KB - 1)),
                            reads=[("win", wpar), ("xn", kb)], writes=[(psk, psl)])
                    if m < 16:
                        db = m
                        P.add("act", lambda e, db=db, pst=pst, psl=psl: e.activation(
                            out=xc_buf[:, db % 2, 3:3 + T], in_=pst[:, psl, :], func=AF.Copy),
                            reads=[(psk, psl)], writes=[("xcmain", db % 2)])
                        if db + 1 < 16:
                            emit_tailin(db + 1)
                        P.add("dve", lambda e, db=db: e.tensor_scalar_mul(
                            out=sp_buf[:, db % 2, :], in0=xc_buf[:, db % 2, 0:T],
                            scalar1=convw_sb[:, 4 * db + 0:4 * db + 1]),
                            reads=[("xcmain", db % 2), ("xcpad", db % 2), ("convw",)],
                            writes=[("sp", db % 2)])
                        P.add("dve", lambda e, db=db: e.scalar_tensor_tensor(
                            out=conv_acc[:, db % 2, :], in0=xc_buf[:, db % 2, 1:1 + T],
                            scalar=convw_sb[:, 4 * db + 1:4 * db + 2],
                            in1=sp_buf[:, db % 2, :], op0=OP.mult, op1=OP.add),
                            reads=[("xcmain", db % 2), ("xcpad", db % 2), ("sp", db % 2), ("convw",)],
                            writes=[("cacc", db % 2)])
                        P.add("dve", lambda e, db=db: e.scalar_tensor_tensor(
                            out=sp_buf[:, db % 2, :], in0=xc_buf[:, db % 2, 2:2 + T],
                            scalar=convw_sb[:, 4 * db + 2:4 * db + 3],
                            in1=conv_acc[:, db % 2, :], op0=OP.mult, op1=OP.add),
                            reads=[("xcmain", db % 2), ("cacc", db % 2), ("convw",)],
                            writes=[("sp", db % 2)])
                        P.add("dve", lambda e, db=db: e.scalar_tensor_tensor(
                            out=conv_acc[:, db % 2, :], in0=xc_buf[:, db % 2, 3:3 + T],
                            scalar=convw_sb[:, 4 * db + 3:4 * db + 4],
                            in1=sp_buf[:, db % 2, :], op0=OP.mult, op1=OP.add),
                            reads=[("xcmain", db % 2), ("sp", db % 2), ("convw",)],
                            writes=[("cacc", db % 2)])
                        if ci % (L // T) != (L // T) - 1:
                            P.add("dve", lambda e, db=db: e.tensor_copy(
                                out=xtail[:, db, :], in_=xc_buf[:, db % 2, T:T + 3]),
                                reads=[("xcmain", db % 2)], writes=[("xtail", db)])
                        P.add("act", lambda e, db=db: e.activation(
                            out=xcv_sb[:, db, :], in_=conv_acc[:, db % 2, :], func=AF.Silu,
                            bias=convb_sb[:, db:db + 1]),
                            reads=[("cacc", db % 2), ("convb",)], writes=[("xcv", db)])
            def emit_zpair_dma(p):
                wpar = p % 2
                view = ap["winB"][16 + 2 * p:18 + 2 * p].rearrange(
                    "s (kb p) m -> p (s kb) m", p=128)
                dma("syncq", wout_sb[:, wpar, :, :], view,
                    reads=[("winB", 16 + 2 * p), ("winB", 17 + 2 * p)],
                    writes=[("wout", wpar)], stream="wout%d" % wpar)

            def emit_zprefetch():
                emit_zpair_dma(0)
                emit_zpair_dma(1)

            def emit_zhalf():
                for m in range(16, 32):
                    p = (m - 16) // 2
                    half = (m - 16) % 2
                    wpar = p % 2
                    pst, psl, psk = WIN_BANKS[m % 4]
                    for kb in range(KB):
                        P.add("pe", lambda e, wpar=wpar, half=half, pst=pst, psl=psl, kb=kb: e.matmul(
                            out=pst[:, psl, :],
                            lhsT=wout_sb[:, wpar, 8 * half + kb, :],
                            rhs=xn_sb[:, kb, :],
                            start=(kb == 0), stop=(kb == KB - 1)),
                            reads=[("wout", wpar), ("xn", kb)], writes=[(psk, psl)])
                    zi = m - 16
                    P.add("act", lambda e, zi=zi, pst=pst, psl=psl: e.activation(
                        out=g_sb[:, zi, :], in_=pst[:, psl, :], func=AF.Silu),
                        reads=[(psk, psl)], writes=[("g", zi)])
                    if half == 1 and p + 2 < 8:
                        emit_zpair_dma(p + 2)

            FFN_BANKS = [(ps_a, 0), (ps_a, 1), (ps_b, 0), (ps_b, 1)]
            FFN_BKEY = [("psa", 0), ("psa", 1), ("psb", 0), ("psb", 1)]

            def ffn_w1(kg, wave, ci=ci, scratch_done=scratch_done):
                for i in range(4):
                    ht = 8 * kg + 4 * wave + i
                    spar = ht % 2
                    dma("syncq", s1_sb[:, spar, :, :], s1R[ht], writes=[("s1", spar)],
                        deps=(scratch_done if ci == 0 else ()), stream="s1_%d" % spar)
                    pst, psl = FFN_BANKS[i]
                    for kb in range(KB):
                        P.add("pe", lambda e, spar=spar, kb=kb, pst=pst, psl=psl: e.matmul(
                            out=pst[:, psl, :], lhsT=s1_sb[:, spar, kb, :],
                            rhs=xn_sb[:, kb, :],
                            start=(kb == 0), stop=(kb == KB - 1)),
                            reads=[("s1", spar), ("xn", kb)], writes=[FFN_BKEY[i]])

            def ffn_gelu(kg, wave):
                for i in range(4):
                    ht = 8 * kg + 4 * wave + i
                    hh = 4 * wave + i
                    pst, psl = FFN_BANKS[i]
                    P.add("act", lambda e, ht=ht, hh=hh, pst=pst, psl=psl: e.activation(
                        out=h1grp[:, hh, :], in_=pst[:, psl, :], func=AF.Gelu,
                        scale=alpha1c[:, ht:ht + 1], bias=b1c_sb[:, ht:ht + 1]),
                        reads=[FFN_BKEY[i], ("alpha1c",), ("b1c",)], writes=[("h1", hh)])

            def ffn_w2(kg, ci=ci, scratch_done=scratch_done):
                for m in range(8):
                    ppar = (8 * kg + m) % 2
                    dma("syncq", s2_sb[:, ppar, :, :],
                        s2R[:, 8 * kg:8 * kg + 8, 128 * m:128 * (m + 1)],
                        writes=[("s2", ppar)], deps=(scratch_done if ci == 0 else ()),
                        stream="s2_%d" % ppar)
                    if True:
                        sl = m % 2
                        for k8 in range(8):
                            P.add("pe", lambda e, ppar=ppar, sl=sl, k8=k8: e.matmul(
                                out=ps_m[:, sl, :],
                                lhsT=s2_sb[:, ppar, k8, :],
                                rhs=h1grp[:, k8, :],
                                start=(k8 == 0), stop=(k8 == 7)),
                                reads=[("s2", ppar)] + [("h1", k8)], writes=[("psm", sl)])
                        ts4 = m % 2
                        if kg == 0:
                            P.add("act", lambda e, m=m, sl=sl, ts4=ts4: e.activation(
                                out=ts_buf[:, ts4, :], in_=ps_m[:, sl, :], func=AF.Identity,
                                scale=alpha2c[:, m:m + 1], bias=b2c_sb[:, m:m + 1]),
                                reads=[("psm", sl), ("alpha2c",), ("b2c",)],
                                writes=[("ts", ts4)])
                        else:
                            P.add("act", lambda e, m=m, sl=sl, ts4=ts4: e.activation(
                                out=ts_buf[:, ts4, :], in_=ps_m[:, sl, :], func=AF.Identity,
                                scale=alpha2c[:, m:m + 1]),
                                reads=[("psm", sl), ("alpha2c",)], writes=[("ts", ts4)])

                        def xadd(m=m, ts4=ts4):
                            P.add("dve", lambda e: e.tensor_tensor(
                                out=x_sb[:, m, :], in0=ts_buf[:, ts4, :], in1=x_sb[:, m, :],
                                op=OP.add),
                                reads=[("ts", ts4), ("x", m)], writes=[("x", m)])
                        pending_adds.append(xadd)
                        if len(pending_adds) >= 2:
                            pending_adds.pop(0)()

            emit_zprefetch()

            for kb16 in range(16):
                P.add("pe", lambda e, kb16=kb16: e.matmul(
                    out=ps_b[0:64, 0, :], lhsT=wx_sb[:, kb16, 0:64], rhs=xcv_sb[:, kb16, :],
                    start=(kb16 == 0), stop=(kb16 == 15)),
                    reads=[("wx",), ("xcv", kb16)], writes=[("psb", 0)])
            for kb16 in range(16):
                P.add("pe", lambda e, kb16=kb16: e.matmul(
                    out=ps_b[0:32, 1, :], lhsT=wx_sb[:, kb16, 64:96], rhs=xcv_sb[:, kb16, :],
                    start=(kb16 == 0), stop=(kb16 == 15)),
                    reads=[("wx",), ("xcv", kb16)], writes=[("psb", 1)])
            P.add("act", lambda e: e.activation(
                out=dtl_sb[:], in_=ps_b[0:64, 0, :], func=AF.Copy),
                reads=[("psb", 0)], writes=[("dtl",)])
            P.add("act", lambda e: e.activation(
                out=bc_sb[:], in_=ps_b[0:32, 1, :], func=AF.Copy),
                reads=[("psb", 1)], writes=[("bc",)])

            def emit_dtx(mdt):
                P.add("dve", lambda e, mdt=mdt: e.tensor_mul(
                    out=dtx_sb[:, mdt, :], in0=dt_sb[:, mdt, :], in1=xcv_sb[:, mdt, :]),
                    reads=[("dt", mdt), ("xcv", mdt)], writes=[("dtx", mdt)])

            for mdt in range(16):
                sl = mdt % 2
                P.add("pe", lambda e, mdt=mdt, sl=sl: e.matmul(
                    out=ps_b[:, sl, :], lhsT=wdt_sb[:, 128 * mdt:128 * (mdt + 1)],
                    rhs=dtl_sb[:], start=True, stop=True),
                    reads=[("wdt",), ("dtl",)], writes=[("psb", sl)])
                P.add("dve", lambda e, mdt=mdt, sl=sl: e.tensor_scalar(
                    out=sp_buf[:, sl, :], in0=ps_b[:, sl, :],
                    scalar1=bdt_sb[:, mdt:mdt + 1], scalar2=30.0,
                    op0=OP.add, op1=OP.min),
                    reads=[("psb", sl), ("bdt",)], writes=[("sp", sl)])
                P.add("act", lambda e, sl=sl: e.activation(
                    out=ln_tmp[:], in_=sp_buf[:, sl, :], func=AF.Exp),
                    reads=[("sp", sl)], writes=[("ln_tmp",)])
                P.add("act", lambda e, mdt=mdt, sl=sl: e.activation(
                    out=dt_sb[:, mdt, :], in_=ln_tmp[:], func=AF.Ln, bias=1.0),
                    reads=[("ln_tmp",)], writes=[("dt", mdt)])
                if mdt >= 1:
                    emit_dtx(mdt - 1)
            emit_dtx(15)

            if DEBUG and ci == 0:
                dmaD("gpq", ap["d_dt"], dt_sb[:], reads=[("dt", z) for z in range(16)])
                dmaD("gpq", ap["d_dtx"], dtx_sb[:], reads=[("dtx", z) for z in range(16)])
                dmaD("gpq", ap["d_bc"], bc_sb[:], reads=[("bc",)])
                dmaD("gpq", ap["d_dtl"], dtl_sb[:], reads=[("dtl",)])

            rp = ci % 2
            wp = 1 - rp
            pending_yacc = None
            pending_adds = []
            if seq_start:
                P.add("dve", lambda e, rp=rp: e.memset(h_state[:, rp, :], 0.0),
                      reads=[], writes=[("hstate", rp)])
                if DEBUG and ci == 0:
                    P.add("dve", lambda e: e.memset(h_guard1[:], 777.0), writes=[("hg1",)])
                    P.add("dve", lambda e: e.memset(h_guard2[:], 777.0), writes=[("hg2",)])
            j = 0

            def emit_bcast(n):
                P.add("pe", lambda e, n=n: e.matmul(
                    out=ps_bc[:, 0, :], lhsT=E_sb[:, 128 * n:128 * (n + 1)],
                    rhs=bc_sb[:], start=True, stop=True),
                    reads=[("E_sb",), ("bc",)], writes=[("psbc", 0)])
                P.add("act", lambda e: e.activation(
                    out=Bbc_sb[:, 0, :], in_=ps_bc[:, 0, :], func=AF.Copy),
                    reads=[("psbc", 0)], writes=[("Bbc", 0)])
                P.add("pe", lambda e, n=n: e.matmul(
                    out=ps_bc[:, 1, :], lhsT=E_sb[:, 128 * (16 + n):128 * (17 + n)],
                    rhs=bc_sb[:], start=True, stop=True),
                    reads=[("E_sb",), ("bc",)], writes=[("psbc", 1)])
                P.add("act", lambda e: e.activation(
                    out=Cbc_sb[:, 0, :], in_=ps_bc[:, 1, :], func=AF.Copy),
                    reads=[("psbc", 1)], writes=[("Cbc", 0)])

            def emit_dA(n, db0, jas):
                for i in range(4):
                    db = db0 + i
                    col = db * 16 + n
                    ja = jas[i]
                    P.add("act", lambda e, db=db, col=col, ja=ja: e.activation(
                        out=dA_buf[:, ja, :], in_=dt_sb[:, db, :], func=AF.Exp,
                        scale=A_sb[:, col:col + 1]),
                        reads=[("dt", db), ("A_sb",)], writes=[("dA", ja)])

            emit_bcast(0)
            emit_dA(0, 0, [0, 1, 2, 3])
            emit_zhalf()
            for n in range(NST):
                npar = n % 2
                if n > 0:
                    emit_bcast(n)
                for dbq in range(DBL // 4):
                    db0 = 4 * dbq
                    hq = dbq % 2
                    jas = [(j + i) % 4 for i in range(4)]
                    if not (n == 0 and dbq == 0):
                        emit_dA(n, db0, jas)
                    P.add("dve", lambda e, db0=db0, npar=npar: e.tensor_tensor(
                        out=dBx_buf[:], in0=dtx_sb[:, db0:db0 + 4, :],
                        in1=Bbc_sb[:, 0, None, :].broadcast_to([128, 4, T]),
                        op=OP.mult),
                        reads=[("dtx", db0), ("Bbc", 0)],
                        writes=[("dBx",)])
                    for i in range(4):
                        col = (db0 + i) * 16 + n
                        ja = jas[i]
                        P.add("dve", lambda e, col=col, ja=ja, hs=4 * hq + i, i=i, rp=rp: e.tensor_tensor_scan(
                            out=H_buf[:, hs, :], data0=dA_buf[:, ja, :],
                            data1=dBx_buf[:, i, :],
                            initial=h_state[:, rp, col:col + 1], op0=OP.mult, op1=OP.add),
                            reads=[("dA", ja), ("dBx",)], writes=[("H", 4 * hq + i), ("hstate", rp)])
                    P.add("dve", lambda e, npar=npar, hq=hq: e.tensor_tensor(
                        out=HC_buf[:], in0=H_buf[:, 4 * hq:4 * hq + 4, :],
                        in1=Cbc_sb[:, 0, None, :].broadcast_to([128, 4, T]),
                        op=OP.mult),
                        reads=[("H", 4 * hq), ("H", 4 * hq + 1), ("H", 4 * hq + 2),
                               ("H", 4 * hq + 3), ("Cbc", 0)],
                        writes=[("HC",)])
                    if n == 0:
                        P.add("dve", lambda e, db0=db0: e.tensor_copy(
                            out=y_sb[:, db0:db0 + 4, :], in_=HC_buf[:]),
                            reads=[("HC",)],
                            writes=[("y", db0), ("y", db0 + 1), ("y", db0 + 2), ("y", db0 + 3)])
                    else:
                        P.add("dve", lambda e, db0=db0: e.tensor_tensor(
                            out=y_sb[:, db0:db0 + 4, :], in0=HC_buf[:],
                            in1=y_sb[:, db0:db0 + 4, :], op=OP.add),
                            reads=[("HC",), ("y", db0), ("y", db0 + 1), ("y", db0 + 2), ("y", db0 + 3)],
                            writes=[("y", db0), ("y", db0 + 1), ("y", db0 + 2), ("y", db0 + 3)])

                    def yacc_state(db0=db0, hq=hq, wp=wp, n=n):
                        for i in range(4):
                            col = (db0 + i) * 16 + n
                            P.add("act", lambda e, col=col, hs=4 * hq + i: e.copy(
                                out=h_state[:, wp, col:col + 1], in_=H_buf[:, hs, T - 1:T]),
                                reads=[("H", 4 * hq + i)], writes=[("hstate", wp)])
                            if DEBUG and ci == 0:
                                P.add("act", lambda e, col=col, hs=4 * hq + i: e.copy(
                                    out=hdbg_sb[:, col:col + 1], in_=H_buf[:, hs, T - 1:T]),
                                    reads=[("H", 4 * hq + i)], writes=[("hdbg", col)])
                    if pending_yacc is not None:
                        pending_yacc()
                    pending_yacc = yacc_state
                    if pending_adds:
                        pending_adds.pop(0)()
                    j += 4
                if n % 4 == 0:
                    ffn_w1(n // 4, 0)
                elif n % 4 == 1:
                    ffn_gelu(n // 4, 0)
                    ffn_w1(n // 4, 1)
                elif n % 4 == 2:
                    ffn_gelu(n // 4, 1)
                else:
                    ffn_w2(n // 4)

            if DEBUG and ci == 0:
                dmaD("gpq", ap["d_y"], y_sb[:], reads=[("y", z) for z in range(16)])
                dmaD("gpq", ap["d_hstate"], h_state[:, 1, :], reads=[("hstate", 1)])
                dmaD("gpq", ap["d_hcols"], hdbg_sb[:], reads=[("hdbg", c) for c in range(256)])
                P.add("dve", lambda e: e.tensor_copy(out=hdbg_sb[:], in_=h_state[:, 1, :]),
                      reads=[("hstate", 1)],
                      writes=[("hsnap",)] + [("hdbg", c) for c in range(256)])
                dmaD("gpq", ap["d_hsnap"], hdbg_sb[:], reads=[("hsnap",)])

            if pending_yacc is not None:
                pending_yacc()
                pending_yacc = None
            for a in pending_adds:
                a()
            pending_adds = []

            for db in range(DBL):
                dp = db % 2
                P.add("dve", lambda e, db=db, dp=dp: e.scalar_tensor_tensor(
                    out=HC_buf[:, dp, :], in0=xcv_sb[:, db, :],
                    scalar=Dp_sb[:, db:db + 1], in1=y_sb[:, db, :],
                    op0=OP.mult, op1=OP.add),
                    reads=[("xcv", db), ("y", db), ("Dp",)], writes=[("HC", dp)])
                P.add("dve", lambda e, db=db, dp=dp: e.tensor_mul(
                    out=y_sb[:, db, :], in0=HC_buf[:, dp, :], in1=g_sb[:, db, :]),
                    reads=[("HC", dp), ("g", db)], writes=[("y", db)])

            if DEBUG and ci == 0:
                dmaD("gpq", ap["d_ymix"], y_sb[:], reads=[("y", z) for z in range(16)])

            if DEBUG:
                dmaD("gpq", ap["d_ymixA"][ci], y_sb[:], reads=[("y", z) for z in range(16)])

            for m in range(8):
                wpar = m % 2
                dma("syncq", wout_sb[:, wpar, :, :], WoutR[m],
                    reads=[("woutB", m)], writes=[("wout", wpar)],
                    stream="wout%d" % wpar)
                for k16 in range(16):
                    P.add("pe", lambda e, m=m, wpar=wpar, k16=k16: e.matmul(
                        out=ps_m[:, m % 2, :], lhsT=wout_sb[:, wpar, k16, :],
                        rhs=y_sb[:, k16, :], start=(k16 == 0), stop=(k16 == 15)),
                        reads=[("wout", wpar), ("y", k16)], writes=[("psm", m % 2)])
                P.add("dve", lambda e, m=m: e.tensor_tensor(
                    out=x_sb[:, m, :], in0=ps_m[:, m % 2, :], in1=x_sb[:, m, :], op=OP.add),
                    reads=[("psm", m % 2), ("x", m)], writes=[("x", m)])

            if DEBUG:
                dmaD("gpq", ap["d_xfinA"][ci], x_sb[:], reads=[("x", m) for m in range(8)])

            for m in range(8):
                dma("syncq", OcR[ci][:, m, :], x_sb[:, m, :],
                    reads=[("x", m)], writes=[("store", ci, m)], stream="store")
                if ci + 1 < NCH:
                    dma("syncq", x_sb[:, m, :], XcR[ci + 1][:, m, :],
                        writes=[("x", m)], stream="xload")
            if DEBUG and ci == NCH - 1:
                P.add("dve", lambda e: e.tensor_copy(out=ln_tmp[:, 0:8], in_=h_guard1[:]),
                      reads=[("hg1",)], writes=[("gcopy", 0)])
                P.add("dve", lambda e: e.tensor_copy(out=ln_tmp[:, 8:16], in_=h_guard2[:]),
                      reads=[("hg2",)], writes=[("gcopy", 1)])
                dmaD("gpq", ap["d_guards"], ln_tmp[:, 0:128],
                    reads=[("gcopy", 0), ("gcopy", 1)])

        totals = plan.finalize()

        sems = {}
        sem_ctx = contextlib.ExitStack()
        es.enter_context(sem_ctx)
        for e in ("pe", "act", "dve", "pool"):
            if totals.get(e):
                sems[e] = sem_ctx.enter_context(nc.semaphore("sem_" + e))
        if "pool" not in sems and any(o.eng == "pool" for o in plan.ops):
            sems["pool"] = sem_ctx.enter_context(nc.semaphore("sem_pool"))
        for s in plan.stream_totals:
            sems[s] = sem_ctx.enter_context(nc.semaphore("sem_" + s))

        def sem_of(op):
            return op.stream if op.eng in ("syncq", "gpq") else op.eng

        block = es.enter_context(nc.Block())

        GROUPS = {"pe": ("pe",), "act": ("act",), "dve": ("dve",),
                  "syncq": ("syncq",), "gpq": ("gpq", "pool")}

        def emit(engname):
            members = GROUPS[engname]

            def skip_dep(op, d):
                if d.eng not in members:
                    return False
                if d.eng not in ("syncq", "gpq") and d.eng == op.eng:
                    return True
                if d.eng == "pool" and op.eng == "gpq":
                    return True
                if d.eng == "gpq" and op.eng == "pool":
                    return False
                if d.eng in ("syncq", "gpq") and op.eng in ("syncq", "gpq"):
                    if sem_of(d) == getattr(op, "stream", None):
                        return True
                    if {getattr(op, "stream", None), sem_of(d)} == {"xload", "store"}:
                        return True
                return False

            def run(engine):
                watermark = {}
                for op in plan.ops:
                    if op.eng not in members:
                        continue
                    needed = {}
                    for d in op.deps:
                        if skip_dep(op, d):
                            continue
                        s = sem_of(d)
                        assert d.val is not None, (d.eng, s)
                        if d.val > needed.get(s, 0):
                            needed[s] = d.val
                    for s, v in sorted(needed.items()):
                        if v > watermark.get(s, 0):
                            engine.wait_ge(sems[s], v)
                            watermark[s] = v
                    inst = op.fn(engine)
                    if op.need:
                        inst.then_inc(sems[sem_of(op)], op.amt)
                if engname == "syncq":
                    engine.wait_ge(sems["store"], plan.stream_totals["store"])
                if engname == "gpq" and "dump" in plan.stream_totals:
                    engine.wait_ge(sems["dump"], plan.stream_totals["dump"])
            return run

        block.tensor(emit("pe"))
        block.scalar(emit("act"))
        block.vector(emit("dve"))
        block.sync(emit("syncq"))
        block.gpsimd(emit("gpq"))

    return nc


def kernel(L0_attn_x, L0_attn_W_in, L0_attn_conv_w, L0_attn_conv_b, L0_attn_W_x,
           L0_attn_W_dt, L0_attn_b_dt, L0_attn_A_log, L0_attn_Dp, L0_attn_W_out,
           L0_ffn_w1, L0_ffn_b1, L0_ffn_w2, L0_ffn_b2):
    f = np.float32
    asc = np.ascontiguousarray

    WinS = asc(L0_attn_W_in.T.reshape(1024, 32, 128).transpose(1, 0, 2), f)
    w1T = asc(L0_ffn_w1.T, f)
    w2T = asc(L0_ffn_w2.T, f)
    w1o = asc(L0_ffn_w1, f)
    w2o = asc(L0_ffn_w2, f)
    WoutS = asc(L0_attn_W_out.T.reshape(2048, 8, 128).transpose(1, 0, 2), f)
    WxTr = asc(L0_attn_W_x.T.reshape(16, 128, 96).transpose(1, 0, 2), f)
    WdtT = asc(L0_attn_W_dt.T, f)
    Ar = asc(L0_attn_A_log.reshape(16, 128, 16).transpose(1, 0, 2).reshape(128, 256), f)
    convw = asc(L0_attn_conv_w.reshape(16, 128, 4).transpose(1, 0, 2).reshape(128, 64), f)
    convb = asc(L0_attn_conv_b.reshape(16, 128).T, f)
    bdt = asc(L0_attn_b_dt.reshape(16, 128).T, f)
    Dpr = asc(L0_attn_Dp.reshape(16, 128).T, f)
    b1r = asc(L0_ffn_b1.reshape(32, 128).T, f)
    b2c = asc(L0_ffn_b2.reshape(8, 128).T, f)
    eye = asc(np.kron(np.eye(32, dtype=f), np.ones((1, 128), f)))
    ones128 = np.ones((128, 1), f)
    ones1x128 = np.ones((1, 128), f)

    shared = dict(WinS=WinS, w1T=w1T, w2T=w2T, w1o=w1o, w2o=w2o, WoutS=WoutS,
                  WxTr=WxTr, WdtT=WdtT, Ar=Ar, convw=convw, convb=convb, bdt=bdt,
                  Dpr=Dpr, b1r=b1r, b2c=b2c, eye=eye, ones128=ones128,
                  ones1x128=ones1x128)

    in_maps = []
    x = np.asarray(L0_attn_x, f)
    for i in range(NCORES):
        xi = x[2 * i:2 * i + 2]
        Xc = asc(xi.reshape(2, 4, T, 1024).transpose(0, 1, 3, 2).reshape(NCH, 1024, T))
        m = dict(shared)
        m["Xc"] = Xc
        in_maps.append(m)

    nc = build_nc()
    res = run_bass_kernel_spmd(nc, in_maps, core_ids=list(range(NCORES)))

    out = np.empty((16, 2048, 1024), f)
    for i in range(NCORES):
        Oc = res.results[i]["Oc"]
        out[2 * i:2 * i + 2] = (
            Oc.reshape(2, 4, 1024, T).transpose(0, 1, 3, 2).reshape(2, 2048, 1024))
    return out


if __name__ == "__main__":
    import sys
    if "build" in sys.argv:
        nc = build_nc()
        print("build OK")
